# version 03 (final)
# baseline (speedup 1.0000x reference)
import numpy as np
from contextlib import ExitStack
import concourse.bass as bass
import concourse.mybir as mybir
from concourse.bass_utils import run_bass_kernel_spmd

F32 = mybir.dt.float32
BF16 = mybir.dt.bfloat16
AF = mybir.ActivationFunctionType
ALU = mybir.AluOpType
AX = mybir.AxisListType

NCORES = 8
T = 4096
C = 1024
NH, HD = 16, 64
NK = 2048
LN_EPS = 1e-5


class Buf:
    __slots__ = ("w", "r", "name")

    def __init__(self, name=""):
        self.w = {}
        self.r = {}
        self.name = name


class Prog:
    CE = ("pe", "act", "dve", "pool", "sp")

    def __init__(self, nc, stack):
        self.nc = nc
        self.stack = stack
        self.semkeys = ["pe", "act", "dve", "pool", "sp", "q_sp", "q_pool", "q_act", "q_dve"]
        self.sem = {k: stack.enter_context(nc.semaphore("S" + k)) for k in self.semkeys}
        self.cnt = {k: 0 for k in self.semkeys}
        self.waited = {e: {k: 0 for k in self.semkeys} for e in self.CE}
        self.ops = {e: [] for e in self.CE}

    def sb(self, name, shape, dtype, st=None):
        t = (st or self.stack).enter_context(self.nc.sbuf_tensor(name, list(shape), dtype))
        return t, Buf(name)

    def ps(self, name, shape, dtype, st=None):
        t = (st or self.stack).enter_context(self.nc.psum_tensor(name, list(shape), dtype))
        return t, Buf(name)

    def emit(self, ek, fn, reads=(), writes=(), dma=None):
        need = {}
        for b in reads:
            for k, v in b.w.items():
                if v > need.get(k, 0):
                    need[k] = v
        for b in writes:
            for k, v in b.w.items():
                if v > need.get(k, 0):
                    need[k] = v
            for k, v in b.r.items():
                if v > need.get(k, 0):
                    need[k] = v
        wt = self.waited[ek]
        for k in sorted(need):
            v = need[k]
            if k == "pe" and ek == "pe":
                continue
            if v <= wt[k]:
                continue
            s = self.sem[k]
            self.ops[ek].append(lambda e, s=s, v=v: e.wait_ge(s, v))
            wt[k] = v
        if dma is None:
            key, inc = ek, 1
        else:
            key, inc = dma, 16
        self.cnt[key] += inc
        tagv = self.cnt[key]
        s = self.sem[key]
        self.ops[ek].append(lambda e, f=fn, s=s, i=inc: f(e).then_inc(s, i))
        for b in writes:
            if tagv > b.w.get(key, 0):
                b.w[key] = tagv
        for b in reads:
            if tagv > b.r.get(key, 0):
                b.r[key] = tagv

    def load(self, dst_ap, src_ap, writes, reads=(), q="sp"):
        self.emit(q, lambda e, o=dst_ap, i=src_ap: e.dma_start(out=o, in_=i),
                  reads=reads, writes=writes, dma="q_" + q)

    def store(self, dst_ap, src_ap, reads, writes=(), q="pool"):
        self.emit(q, lambda e, o=dst_ap, i=src_ap: e.dma_start(out=o, in_=i),
                  reads=reads, writes=writes, dma="q_" + q)

    def mm(self, out_ap, lhsT_ap, rhs_ap, reads, psum, start, stop):
        self.emit("pe", lambda e, o=out_ap, l=lhsT_ap, r=rhs_ap, st=start, sp=stop:
                  e.matmul(o, l, r, start=st, stop=sp), reads=reads, writes=[psum])

    def trans(self, out_ap, in_ap, ident_ap, reads, psum):
        self.emit("pe", lambda e, o=out_ap, i=in_ap, d=ident_ap: e.transpose(o, i, d),
                  reads=reads, writes=[psum])

    def act(self, out_ap, in_ap, func, reads, writes, bias=0.0, scale=1.0, eng="act"):
        self.emit(eng, lambda e, o=out_ap, i=in_ap, f=func, b=bias, s=scale:
                  e.activation(o, i, f, bias=b, scale=s), reads=reads, writes=writes)

    def tt(self, out_ap, in0_ap, in1_ap, op, reads, writes, eng="dve"):
        self.emit(eng, lambda e, o=out_ap, a=in0_ap, b=in1_ap, p=op:
                  e.tensor_tensor(o, a, b, p), reads=reads, writes=writes)

    def ts(self, out_ap, in0_ap, s1, s2, op0, op1, reads, writes, eng="dve"):
        if s2 is None:
            self.emit(eng, lambda e, o=out_ap, a=in0_ap, s=s1, p=op0:
                      e.tensor_single_scalar(o, a, s, p), reads=reads, writes=writes)
        else:
            self.emit(eng, lambda e, o=out_ap, a=in0_ap, x=s1, y=s2, p0=op0, p1=op1:
                      e.tensor_scalar(o, a, x, y, p0, p1), reads=reads, writes=writes)

    def stt(self, out_ap, in0_ap, scalar, in1_ap, op0, op1, reads, writes, eng="dve"):
        self.emit(eng, lambda e, o=out_ap, a=in0_ap, s=scalar, b=in1_ap, p0=op0, p1=op1:
                  e.scalar_tensor_tensor(o, a, s, b, p0, p1), reads=reads, writes=writes)

    def red(self, out_ap, in_ap, op, reads, writes, eng="dve", negate=False):
        self.emit(eng, lambda e, o=out_ap, i=in_ap, p=op, n=negate:
                  e.tensor_reduce(o, i, AX.X, p, negate=n), reads=reads, writes=writes)

    def copy(self, out_ap, in_ap, reads, writes, eng="dve"):
        self.emit(eng, lambda e, o=out_ap, i=in_ap: e.tensor_copy(o, i),
                  reads=reads, writes=writes)

    def recip(self, out_ap, in_ap, reads, writes):
        self.emit("dve", lambda e, o=out_ap, i=in_ap: e.reciprocal(o, i),
                  reads=reads, writes=writes)

    def memset(self, ap, val, writes, eng="dve"):
        self.emit(eng, lambda e, a=ap, v=val: e.memset(a, v), reads=(), writes=writes)

    def barrier(self):
        for ek in self.CE:
            wt = self.waited[ek]
            for k in self.semkeys:
                v = self.cnt[k]
                if k == ek:
                    continue
                if v > wt[k]:
                    s = self.sem[k]
                    self.ops[ek].append(lambda e, s=s, v=v: e.wait_ge(s, v))
                    wt[k] = v

    def drain(self):
        wt = self.waited["sp"]
        for k in ["q_sp", "q_pool", "q_act", "q_dve", "pe", "act", "dve", "pool"]:
            v = self.cnt[k]
            if v > wt[k]:
                s = self.sem[k]
                self.ops["sp"].append(lambda e, s=s, v=v: e.wait_ge(s, v))
                wt[k] = v

    def finalize(self):
        with self.nc.Block() as blk:
            sections = {"pe": blk.tensor, "act": blk.scalar, "dve": blk.vector,
                        "pool": blk.gpsimd, "sp": blk.sync}
            for ek in self.CE:
                ops = self.ops[ek]

                def body(eng, ops=ops):
                    for f in ops:
                        f(eng)
                sections[ek](body)


def alloc_consts(nc, P, st):
    ident_f, ident_f_b = P.sb("ident_f", [128, 128], F32, st)
    ident_b, ident_b_b = P.sb("ident_b", [128, 128], BF16, st)
    ones_f, ones_f_b = P.sb("ones_f", [128, 128], F32, st)
    ones_b, ones_b_b = P.sb("ones_b", [128, 128], BF16, st)
    idd = nc.inline_tensor(np.eye(128, dtype=np.float32), "identnp")
    P.load(ident_f[:], idd.ap(), writes=[ident_f_b])
    P.copy(ident_b[:], ident_f[:], reads=[ident_f_b], writes=[ident_b_b], eng="pool")
    P.memset(ones_f[:], 1.0, writes=[ones_f_b], eng="pool")
    P.copy(ones_b[:], ones_f[:], reads=[ones_f_b], writes=[ones_b_b], eng="pool")
    cpool, cpool_b = P.sb("cpool", [128, 4], F32, st)
    P.memset(cpool[:, 0:1], LN_EPS, writes=[cpool_b], eng="pool")
    P.memset(cpool[:, 1:2], 1e-6, writes=[cpool_b], eng="pool")
    P.memset(cpool[:, 2:3], 1e-7, writes=[cpool_b], eng="pool")
    P.memset(cpool[:, 3:4], 0.0, writes=[cpool_b], eng="pool")
    return dict(ident_f=(ident_f, ident_f_b), ident_b=(ident_b, ident_b_b),
                ones_f=(ones_f, ones_f_b), ones_b=(ones_b, ones_b_b),
                cpool=(cpool, cpool_b))


class LNer:

    def __init__(self, P, cst, ptb, st, prefix="ln"):
        self.P, self.cst, self.ptb = P, cst, ptb
        self.xt, self.xt_b, self.xn, self.xn_b = [], [], [], []
        for i in range(2):
            a, ab = P.sb(f"{prefix}_xt{i}", [128, C], F32, st)
            self.xt.append(a)
            self.xt_b.append(ab)
            a, ab = P.sb(f"{prefix}_xn{i}", [128, C], BF16, st)
            self.xn.append(a)
            self.xn_b.append(ab)
        self.sq, self.sq_b = P.sb(f"{prefix}_sq", [128, C], F32, st)
        self.st_, self.st_b = P.sb(f"{prefix}_stats", [128, 8], F32, st)

    def tile(self, x_row_ap, xnT_r, col0, i, xnT_b, keep_x=None):
        P, cst = self.P, self.cst
        pt, pt_b = self.ptb
        xt, xt_b = self.xt[i % 2], self.xt_b[i % 2]
        xn, xn_b = self.xn[i % 2], self.xn_b[i % 2]
        stt_, stt_b = self.st_, self.st_b
        P.load(xt[:], x_row_ap, writes=[xt_b])
        if keep_x is not None:
            P.copy(keep_x[0], xt[:], reads=[xt_b], writes=[keep_x[1]], eng="pool")
        P.red(stt_[:, 0:1], xt[:], ALU.add, reads=[xt_b], writes=[stt_b])
        P.act(self.sq[:], xt[:], AF.Square, reads=[xt_b], writes=[self.sq_b])
        P.red(stt_[:, 1:2], self.sq[:], ALU.add, reads=[self.sq_b], writes=[stt_b])
        P.ts(stt_[:, 2:3], stt_[:, 0:1], 1.0 / C, None, ALU.mult, None,
             reads=[stt_b], writes=[stt_b])
        P.tt(stt_[:, 3:4], stt_[:, 2:3], stt_[:, 2:3], ALU.mult,
             reads=[stt_b], writes=[stt_b])
        P.ts(stt_[:, 4:5], stt_[:, 1:2], 1.0 / C, stt_[:, 3:4], ALU.mult, ALU.subtract,
             reads=[stt_b], writes=[stt_b])
        P.act(stt_[:, 5:6], stt_[:, 4:5], AF.Sqrt, reads=[stt_b, cst["cpool"][1]],
              writes=[stt_b], bias=cst["cpool"][0][:, 0:1])
        P.recip(stt_[:, 6:7], stt_[:, 5:6], reads=[stt_b], writes=[stt_b])
        P.ts(xn[:], xt[:], stt_[:, 2:3], stt_[:, 6:7], ALU.subtract, ALU.mult,
             reads=[xt_b, stt_b], writes=[xn_b])
        for c in range(8):
            P.trans(pt[:, c * 128:(c + 1) * 128], xn[:, c * 128:(c + 1) * 128],
                    cst["ident_b"][0][:], reads=[xn_b, cst["ident_b"][1]], psum=pt_b)
        P.copy(xnT_r[:, :, col0:col0 + 128],
               pt[:].rearrange("p (c t) -> p c t", t=128),
               reads=[pt_b], writes=[xnT_b], eng="dve")


def emit_ln_to_xnT(P, cst, ptb, x_dram, xnT, xnT_b, st, ntiles=32, prefix="ln"):
    TT = ntiles * 128
    ln = LNer(P, cst, ptb, st, prefix)
    xnT_r = xnT.ap().rearrange("p (c t) -> p c t", t=TT)
    for tt in range(ntiles):
        ln.tile(x_dram[tt * 128:(tt + 1) * 128, :], xnT_r, tt * 128, tt, xnT_b)


def load_w_bf16(P, w_dram_ap, wbf, wbf_b, stage, stage_b, rows, cols, halves=2):
    rc = rows // 128
    src = w_dram_ap.rearrange("(r p) n -> p r n", p=128)
    step = max(1, rc // halves)
    r = 0
    while r < rc:
        r2 = min(rc, r + step)
        for j in range(r, r2):
            P.load(stage[:, (j - r) * cols:(j - r + 1) * cols], src[:, j, :],
                   writes=[stage_b])
        P.copy(wbf[:, r * cols:r2 * cols], stage[:, 0:(r2 - r) * cols],
               reads=[stage_b], writes=[wbf_b], eng="pool")
        r = r2


def emit_l0_attn(nc, P, cst, pb, ptb, io):
    x_d = io["x"]
    wT_d = {k: io[k + "wT"] for k in ("q", "k", "v", "o")}
    qb_d = io["qb"]
    kb_d = io["kb"]
    convw_d = io["convw"]
    convb_d = io["convb"]
    vbt_d = io["vbt"]
    obt_d = io["obt"]
    y1_d = io["y1"]
    qT_hbm = nc.dram_tensor("qT_hbm", [C, T], BF16, kind="Internal").ap()
    xdT_hbm = nc.dram_tensor("xdT_hbm", [C, NK], BF16, kind="Internal").ap()

    with ExitStack() as sA:
        qb, qb_b = P.sb("qb_s", [128, 8], F32, sA)
        kb, kb_b = P.sb("kb_s", [128, 8], F32, sA)
        cw, cw_b = P.sb("cw_s", [128, 24], F32, sA)
        cb, cb_b = P.sb("cb_s", [128, 8], F32, sA)
        P.load(qb[:], qb_d, writes=[qb_b])
        P.load(kb[:], kb_d, writes=[kb_b])
        P.load(cw[:], convw_d, writes=[cw_b])
        P.load(cb[:], convb_d, writes=[cb_b])

        with ExitStack() as s1:
            xnT, xnT_b = P.sb("xnT", [128, 8 * T], BF16, s1)
            with ExitStack() as s1a:
                emit_ln_to_xnT(P, cst, ptb, x_d, xnT, xnT_b, s1a)
                P.barrier()
            with ExitStack() as s1b:
                t1, t1_b = P.sb("cv_t1", [128, NK], F32, s1b)
                t2, t2_b = P.sb("cv_t2", [128, NK], F32, s1b)
                xdc, xdc_b = P.sb("cv_xd", [128, NK], BF16, s1b)
                xdT_r = xdT_hbm.rearrange("(c p) t -> p c t", p=128)
                for c in range(8):
                    xc = xnT[:, c * T:(c + 1) * T]
                    P.ts(t1[:], xc[0:128, 0:T:2], cw[:, c * 3 + 1:c * 3 + 2],
                         cb[:, c:c + 1], ALU.mult, ALU.add,
                         reads=[xnT_b, cw_b, cb_b], writes=[t1_b])
                    P.stt(t2[:], xc[0:128, 1:T:2], cw[:, c * 3 + 2:c * 3 + 3], t1[:],
                          ALU.mult, ALU.add, reads=[xnT_b, cw_b, t1_b], writes=[t2_b])
                    P.stt(xdc[0:128, 1:NK], xc[0:128, 1:T - 2:2], cw[:, c * 3:c * 3 + 1],
                          t2[:, 1:NK], ALU.mult, ALU.add,
                          reads=[xnT_b, cw_b, t2_b], writes=[xdc_b])
                    P.copy(xdc[0:128, 0:1], t2[:, 0:1], reads=[t2_b], writes=[xdc_b])
                    P.store(xdT_r[:, c, :], xdc[:], reads=[xdc_b])
                P.barrier()
            with ExitStack() as s1c:
                wstage, wstage_b = P.sb("wstage", [128, 4 * C], F32, s1c)
                wbf, wbf_b = P.sb("wbf", [128, 8 * C], BF16, s1c)
                load_w_bf16(P, wT_d["q"], wbf, wbf_b, wstage, wstage_b, C, C)
                qTc, qTc_b = P.sb("qTc", [128, 8 * 512], BF16, s1c)
                qT_r = qT_hbm.rearrange("(i p) t -> p i t", p=128)
                for qc in range(8):
                    for oc in range(8):
                        p, p_b = pb[oc % 2]
                        for ic in range(8):
                            P.mm(p[:, 0:512],
                                 wbf[:, ic * C + oc * 128: ic * C + oc * 128 + 128],
                                 xnT[:, ic * T + qc * 512: ic * T + (qc + 1) * 512],
                                 reads=[wbf_b, xnT_b], psum=p_b,
                                 start=(ic == 0), stop=(ic == 7))
                        P.act(qTc[:, oc * 512:(oc + 1) * 512], p[:, 0:512], AF.Identity,
                              reads=[p_b, qb_b], writes=[qTc_b], bias=qb[:, oc:oc + 1])
                    for oc in range(8):
                        P.store(qT_r[:, oc, qc * 512:(qc + 1) * 512],
                                qTc[:, oc * 512:(oc + 1) * 512], reads=[qTc_b])
                P.barrier()
        P.barrier()

        with ExitStack() as s2:
            kT, kT_b = P.sb("kT", [128, 8 * NK], BF16, s2)
            vnat, vnat_b = P.sb("vnat", [128, 16 * C], BF16, s2)
            owTb, owTb_b = P.sb("owTb", [128, 8 * C], BF16, s2)
            obt, obt_b = P.sb("obt_s", [128, C], F32, s2)
            P.load(obt[:], obt_d, writes=[obt_b])
            with ExitStack() as s2a:
                xdTs, xdTs_b = P.sb("xdTs", [128, 8 * NK], BF16, s2a)
                xdT_r = xdT_hbm.rearrange("(c p) t -> p c t", p=128)
                for c in range(8):
                    P.load(xdTs[:, c * NK:(c + 1) * NK], xdT_r[:, c, :], writes=[xdTs_b])
                wstage, wstage_b = P.sb("wstage2", [128, 4 * C], F32, s2a)
                wbf, wbf_b = P.sb("wbf2", [128, 8 * C], BF16, s2a)
                vbt, vbt_b = P.sb("vbt_s", [128, C], F32, s2a)
                P.load(vbt[:], vbt_d, writes=[vbt_b])
                load_w_bf16(P, wT_d["k"], wbf, wbf_b, wstage, wstage_b, C, C)
                for oc in range(8):
                    for kc in range(4):
                        p, p_b = pb[kc % 2]
                        for ic in range(8):
                            P.mm(p[:, 0:512],
                                 wbf[:, ic * C + oc * 128: ic * C + oc * 128 + 128],
                                 xdTs[:, ic * NK + kc * 512: ic * NK + (kc + 1) * 512],
                                 reads=[wbf_b, xdTs_b], psum=p_b,
                                 start=(ic == 0), stop=(ic == 7))
                        P.act(kT[:, oc * NK + kc * 512: oc * NK + (kc + 1) * 512],
                              p[:, 0:512], AF.Identity, reads=[p_b, kb_b], writes=[kT_b],
                              bias=kb[:, oc:oc + 1])
                load_w_bf16(P, wT_d["v"], wbf, wbf_b, wstage, wstage_b, C, C)
                for kt in range(16):
                    for oc in range(2):
                        p, p_b = pb[2 + (oc % 2)]
                        for ic in range(8):
                            P.mm(p[:, 0:512],
                                 xdTs[:, ic * NK + kt * 128: ic * NK + kt * 128 + 128],
                                 wbf[:, ic * C + oc * 512: ic * C + (oc + 1) * 512],
                                 reads=[wbf_b, xdTs_b], psum=p_b,
                                 start=(ic == 0), stop=(ic == 7))
                        P.tt(vnat[:, kt * C + oc * 512: kt * C + (oc + 1) * 512],
                             p[:, 0:512], vbt[:, oc * 512:(oc + 1) * 512], ALU.add,
                             reads=[p_b, vbt_b], writes=[vnat_b])
                load_w_bf16(P, wT_d["o"], owTb, owTb_b, wstage, wstage_b, C, C)
                P.barrier()

            with ExitStack() as s3:
                qTc, qTc_b, Es, Es_b = [], [], [], []
                for i in range(2):
                    a, ab = P.sb(f"at_qTc{i}", [128, 8 * 512], BF16, s3)
                    qTc.append(a)
                    qTc_b.append(ab)
                    a, ab = P.sb(f"at_E{i}", [128, 512], BF16, s3)
                    Es.append(a)
                    Es_b.append(ab)
                oT, oT_b = P.sb("at_oT", [128, 8 * 512], BF16, s3)
                rden, rden_b = P.sb("at_rden", [1, 512], F32, s3)
                bc, bc_b = P.sb("at_bc", [128, 512], F32, s3)
                xres, xres_b = P.sb("at_xres", [128, C], F32, s3)
                tsb, tsb_b = P.sb("at_tsb", [128, 512], F32, s3)
                y1sb, y1sb_b = [], []
                for i in range(2):
                    a, ab = P.sb(f"at_y1{i}", [128, C], F32, s3)
                    y1sb.append(a)
                    y1sb_b.append(ab)
                qT_r = qT_hbm.rearrange("(i p) t -> p i t", p=128)
                ps0, ps0_b = pb[0]
                ps1, ps1_b = pb[1]
                po, po_b = pb[2]
                pd, pd_b = pb[3]
                pbx, pbx_b = pb[4]
                py, py_b = pb[5]
                onesb = cst["ones_b"]
                onesf = cst["ones_f"]
                for qc in range(8):
                    qq = qTc[qc % 2]
                    qq_b = qTc_b[qc % 2]
                    for ic in range(8):
                        P.load(qq[:, ic * 512:(ic + 1) * 512],
                               qT_r[:, ic, qc * 512:(qc + 1) * 512], writes=[qq_b])
                    for h in range(NH):
                        base = (h % 2) * 64
                        icq = h // 2
                        qsl = qq[base:base + 64, icq * 512:(icq + 1) * 512]
                        sbank = [(ps0, ps0_b), (ps1, ps1_b)]
                        P.mm(sbank[0][0][:, 0:512],
                             kT[base:base + 64, icq * NK: icq * NK + 128], qsl,
                             reads=[kT_b, qq_b], psum=sbank[0][1], start=True, stop=True)
                        for kc in range(16):
                            sp, sp_b = sbank[kc % 2]
                            P.act(Es[kc % 2][:], sp[:, 0:512], AF.Exp,
                                  reads=[sp_b], writes=[Es_b[kc % 2]], scale=0.125)
                            if kc < 15:
                                sp2, sp2_b = sbank[(kc + 1) % 2]
                                P.mm(sp2[:, 0:512],
                                     kT[base:base + 64,
                                        icq * NK + (kc + 1) * 128: icq * NK + (kc + 2) * 128],
                                     qsl, reads=[kT_b, qq_b], psum=sp2_b,
                                     start=True, stop=True)
                            P.mm(po[base:base + 64, 0:512],
                                 vnat[:, kc * C + h * HD: kc * C + (h + 1) * HD],
                                 Es[kc % 2][:], reads=[vnat_b, Es_b[kc % 2]], psum=po_b,
                                 start=(kc == 0), stop=(kc == 15))
                            P.mm(pd[0:1, 0:512], onesb[0][:, 0:1], Es[kc % 2][:],
                                 reads=[onesb[1], Es_b[kc % 2]], psum=pd_b,
                                 start=(kc == 0), stop=(kc == 15))
                        P.recip(rden[:], pd[0:1, 0:512], reads=[pd_b], writes=[rden_b])
                        P.mm(pbx[:, 0:512], onesf[0][0:1, :], rden[:],
                             reads=[onesf[1], rden_b], psum=pbx_b, start=True, stop=True)
                        P.copy(bc[:], pbx[:, 0:512], reads=[pbx_b], writes=[bc_b],
                               eng="dve")
                        P.tt(oT[base:base + 64, icq * 512:(icq + 1) * 512],
                             po[base:base + 64, 0:512], bc[base:base + 64, :], ALU.mult,
                             reads=[po_b, bc_b], writes=[oT_b])
                    for t2_ in range(4):
                        P.load(xres[:], x_d[qc * 512 + t2_ * 128: qc * 512 + (t2_ + 1) * 128, :],
                               writes=[xres_b])
                        yb = y1sb[t2_ % 2]
                        yb_b = y1sb_b[t2_ % 2]
                        for oc in range(2):
                            for ic in range(8):
                                P.mm(py[:, 0:512],
                                     oT[:, ic * 512 + t2_ * 128: ic * 512 + t2_ * 128 + 128],
                                     owTb[:, ic * C + oc * 512: ic * C + (oc + 1) * 512],
                                     reads=[oT_b, owTb_b], psum=py_b,
                                     start=(ic == 0), stop=(ic == 7))
                            P.tt(tsb[:], py[:, 0:512], obt[:, oc * 512:(oc + 1) * 512],
                                 ALU.add, reads=[py_b, obt_b], writes=[tsb_b])
                            P.tt(yb[:, oc * 512:(oc + 1) * 512], tsb[:],
                                 xres[:, oc * 512:(oc + 1) * 512], ALU.add,
                                 reads=[tsb_b, xres_b], writes=[yb_b])
                        P.store(y1_d[qc * 512 + t2_ * 128: qc * 512 + (t2_ + 1) * 128, :],
                                yb[:], reads=[yb_b])
                P.barrier()


def prep_l0_attn_inputs(inp, b):
    col8 = lambda v: np.ascontiguousarray(v.reshape(8, 128).T)
    return {
        "x": np.ascontiguousarray(inp["L0_attn_x"][b]),
        "qwT": np.ascontiguousarray(inp["L0_attn_q_w"].T),
        "kwT": np.ascontiguousarray(inp["L0_attn_k_w"].T),
        "vwT": np.ascontiguousarray(inp["L0_attn_v_w"].T),
        "owT": np.ascontiguousarray(inp["L0_attn_out_w"].T),
        "qb": col8(inp["L0_attn_q_b"]),
        "kb": col8(inp["L0_attn_k_b"]),
        "convw": np.ascontiguousarray(
            inp["L0_attn_conv_w"].reshape(8, 128, 3).transpose(1, 0, 2).reshape(128, 24)),
        "convb": col8(inp["L0_attn_conv_b"]),
        "vbt": np.ascontiguousarray(np.broadcast_to(inp["L0_attn_v_b"], (128, C))),
        "obt": np.ascontiguousarray(np.broadcast_to(inp["L0_attn_out_b"], (128, C))),
    }


def emit_cast_pass(P, src_ap, dst_ap, rows, cols, stage, stage_b, stgb, stgb_b):
    rc = rows // 128
    s = src_ap.rearrange("(r p) n -> r p n", p=128)
    d = dst_ap.rearrange("(r p) n -> r p n", p=128)
    for r in range(rc):
        P.load(stage[:, 0:cols], s[r], writes=[stage_b])
        P.copy(stgb[:, 0:cols], stage[:, 0:cols], reads=[stage_b], writes=[stgb_b],
               eng="pool")
        P.store(d[r], stgb[:, 0:cols], reads=[stgb_b])


def emit_moe_ffn(nc, P, cst, pb, ptb, layer, io):
    E = 8
    H = 1408 if layer == 0 else 4096
    HS = 2816 if layer == 0 else 4096
    HC = H // 128
    HSC = HS // 128
    y_in = io["y_in"]
    wg_d = io["wg"]
    w1_d = io["w1"]
    w2m_d = io["w2m"]
    w3_d = io["w3"]
    ws1_d = io["ws1"]
    ws2_d = io["ws2"]
    ws3_d = io["ws3"]
    y_out = io["y_out"]
    w1b = nc.dram_tensor("w1b", [E, C, H], BF16, kind="Internal").ap()
    w2mb = nc.dram_tensor("w2mb", [E, C, H], BF16, kind="Internal").ap()
    w3b = nc.dram_tensor("w3b", [E, H, C], BF16, kind="Internal").ap()
    ws1b = nc.dram_tensor("ws1b", [C, HS], BF16, kind="Internal").ap()
    ws2b = nc.dram_tensor("ws2b", [C, HS], BF16, kind="Internal").ap()
    ws3b = nc.dram_tensor("ws3b", [HS, C], BF16, kind="Internal").ap()

    with ExitStack() as sA:
        with ExitStack() as sW:
            stage, stage_b = P.sb("cw_stage", [128, max(H, HS, C)], F32, sW)
            stgb, stgb_b = P.sb("cw_stgb", [128, max(H, HS, C)], BF16, sW)
            for e in range(E):
                emit_cast_pass(P, w1_d[e], w1b[e], C, H, stage, stage_b, stgb, stgb_b)
                emit_cast_pass(P, w2m_d[e], w2mb[e], C, H, stage, stage_b, stgb, stgb_b)
                emit_cast_pass(P, w3_d[e], w3b[e], H, C, stage, stage_b, stgb, stgb_b)
            emit_cast_pass(P, ws1_d, ws1b, C, HS, stage, stage_b, stgb, stgb_b)
            emit_cast_pass(P, ws2_d, ws2b, C, HS, stage, stage_b, stgb, stgb_b)
            emit_cast_pass(P, ws3_d, ws3b, HS, C, stage, stage_b, stgb, stgb_b)
            P.barrier()

        wg, wg_b = P.sb("wg_s", [128, 8 * E], F32, sA)
        P.load(wg[:].rearrange("p (i e) -> p i e", e=E),
               wg_d.rearrange("(i p) e -> p i e", p=128), writes=[wg_b])
        wgb, wgb_b = P.sb("wg_b16", [128, 8 * E], BF16, sA)
        P.copy(wgb[:], wg[:], reads=[wg_b], writes=[wgb_b], eng="pool")

        ln = LNer(P, cst, ptb, sA, "ffn_ln")
        xnT, xnT_b = P.sb("ffn_xnT", [128, 8 * 512], BF16, sA)
        xnT_r = xnT.ap().rearrange("p (c t) -> p c t", t=512)
        yacc, yacc_b = [], []
        for i in range(2):
            a, ab = P.sb(f"ffn_yacc{i}", [128, 4 * C], F32, sA)
            yacc.append(a)
            yacc_b.append(ab)
        lg, lg_b = P.sb("ffn_lg", [128, 4 * 24], F32, sA)
        gsc, gsc_b = P.sb("ffn_gsc", [128, 4 * E], F32, sA)
        SLAB = ((max(HC, HSC) + 3) // 4) * 128
        w1sl = [P.sb(f"ffn_w1s{j}", [128, 8 * SLAB], BF16, sA) for j in range(2)]
        w2sl = [P.sb(f"ffn_w2s{j}", [128, 8 * SLAB], BF16, sA) for j in range(2)]
        hT, hT_b = P.sb("ffn_hT", [128, max(HC, HSC) * 512], BF16, sA)
        gbuf, gbuf_b = P.sb("ffn_gbuf", [128, 512], BF16, sA)
        w3s, w3s_b = [], []
        for i in range(2):
            a, ab = P.sb(f"ffn_w3s{i}", [128, C], BF16, sA)
            w3s.append(a)
            w3s_b.append(ab)

        w1r = w1b.rearrange("e (i p) h -> e i p h", p=128)
        w2r = w2mb.rearrange("e (i p) h -> e i p h", p=128)
        w3r = w3b.rearrange("e (h p) c -> e h p c", p=128)
        ws1r = ws1b.rearrange("(i p) h -> i p h", p=128)
        ws2r = ws2b.rearrange("(i p) h -> i p h", p=128)
        ws3r = ws3b.rearrange("(h p) c -> h p c", p=128)

        for tc in range(8):
            cur = 0
            for i in range(4):
                P.load(yacc[0][:, i * C:(i + 1) * C],
                       y_in[tc * 512 + i * 128: tc * 512 + (i + 1) * 128, :],
                       writes=[yacc_b[0]])
            for i in range(4):
                ln.tile(y_in[tc * 512 + i * 128: tc * 512 + (i + 1) * 128, :],
                        xnT_r, i * 128, i, xnT_b)
            for i in range(4):
                pr, pr_b = pb[4]
                for ic in range(8):
                    P.mm(pr[:, 0:E], xnT[:, ic * 512 + i * 128: ic * 512 + i * 128 + 128],
                         wgb[:, ic * E:(ic + 1) * E], reads=[xnT_b, wgb_b], psum=pr_b,
                         start=(ic == 0), stop=(ic == 7))
                lgi = lg[:, i * 24:(i + 1) * 24]
                logit = lgi[:, 0:E]
                scr = lgi[:, 8:16]
                m1 = lgi[:, 16:17]
                m2 = lgi[:, 17:18]
                c18, c19, c20, c21, c22 = (lgi[:, j:j + 1] for j in range(18, 23))
                eq1 = gsc[:, i * E:(i + 1) * E]
                P.copy(logit, pr[:, 0:E], reads=[pr_b], writes=[lg_b], eng="dve")
                P.red(m1, logit, ALU.max, reads=[lg_b], writes=[lg_b])
                P.ts(eq1, logit, m1, None, ALU.is_equal, None,
                     reads=[lg_b], writes=[gsc_b])
                P.stt(scr, eq1, -1e30, logit, ALU.mult, ALU.add,
                      reads=[gsc_b, lg_b], writes=[lg_b])
                P.red(m2, scr, ALU.max, reads=[lg_b], writes=[lg_b])
                if layer == 0:
                    P.tt(c18, m2, m1, ALU.subtract, reads=[lg_b], writes=[lg_b])
                    P.act(c19, c18, AF.Exp, reads=[lg_b], writes=[lg_b])
                    P.ts(c20, c19, 1.0, None, ALU.add, None, reads=[lg_b],
                         writes=[lg_b])
                    P.recip(c21, c20, reads=[lg_b], writes=[lg_b])
                    P.tt(c22, c19, c21, ALU.mult, reads=[lg_b], writes=[lg_b])
                    P.ts(logit, scr, m2, None, ALU.is_equal, None,
                         reads=[lg_b], writes=[lg_b])
                    P.ts(eq1, eq1, c21, None, ALU.mult, None,
                         reads=[gsc_b, lg_b], writes=[gsc_b])
                    P.stt(gsc[:, i * E:(i + 1) * E], logit, c22, eq1,
                          ALU.mult, ALU.add, reads=[lg_b, gsc_b], writes=[gsc_b])
                else:
                    P.ts(scr, scr, m2, None, ALU.is_equal, None,
                         reads=[lg_b], writes=[lg_b])
                    P.tt(eq1, eq1, scr, ALU.add, reads=[gsc_b, lg_b], writes=[gsc_b])
                    P.ts(c18, m1, -1.0, None, ALU.mult, None, reads=[lg_b],
                         writes=[lg_b])
                    P.act(scr, logit, AF.Exp, reads=[lg_b], writes=[lg_b], bias=c18)
                    P.red(c19, scr, ALU.add, reads=[lg_b], writes=[lg_b])
                    P.recip(c20, c19, reads=[lg_b], writes=[lg_b])
                    P.ts(scr, scr, c20, 0.5, ALU.mult, ALU.mult,
                         reads=[lg_b], writes=[lg_b])
                    P.tt(gsc[:, i * E:(i + 1) * E], eq1, scr, ALU.mult,
                         reads=[gsc_b, lg_b], writes=[gsc_b])

            for e in range(E + 1):
                nhc = HC if e < E else HSC
                npc = 4 if nhc >= 4 else nhc
                base, rem, off = nhc // npc, nhc % npc, 0
                pieces = []
                for i_ in range(npc):
                    sz = base + (1 if i_ < rem else 0)
                    pieces.append((off, sz))
                    off += sz
                for pi, (h0, h1) in enumerate(pieces):
                    w1s, w1s_b = w1sl[pi % 2]
                    w2s, w2s_b = w2sl[pi % 2]
                    for ic in range(8):
                        if e < E:
                            s1 = w1r[e, ic, :, h0 * 128:(h0 + h1) * 128]
                            s2 = w2r[e, ic, :, h0 * 128:(h0 + h1) * 128]
                        else:
                            s1 = ws1r[ic, :, h0 * 128:(h0 + h1) * 128]
                            s2 = ws2r[ic, :, h0 * 128:(h0 + h1) * 128]
                        P.load(w1s[:, ic * SLAB:ic * SLAB + h1 * 128], s1,
                               writes=[w1s_b])
                        P.load(w2s[:, ic * SLAB:ic * SLAB + h1 * 128], s2,
                               writes=[w2s_b], q="pool")
                    for hh in range(h1):
                        hc = h0 + hh
                        pg, pg_b = pb[4]
                        pu, pu_b = pb[5]
                        for ic in range(8):
                            P.mm(pg[:, 0:512], w1s[:, ic * SLAB + hh * 128: ic * SLAB + hh * 128 + 128],
                                 xnT[:, ic * 512:(ic + 1) * 512], reads=[w1s_b, xnT_b],
                                 psum=pg_b, start=(ic == 0), stop=(ic == 7))
                        for ic in range(8):
                            P.mm(pu[:, 0:512], w2s[:, ic * SLAB + hh * 128: ic * SLAB + hh * 128 + 128],
                                 xnT[:, ic * 512:(ic + 1) * 512], reads=[w2s_b, xnT_b],
                                 psum=pu_b, start=(ic == 0), stop=(ic == 7))
                        P.act(gbuf[:], pg[:, 0:512], AF.Silu, reads=[pg_b],
                              writes=[gbuf_b])
                        P.tt(hT[:, hc * 512:(hc + 1) * 512], pu[:, 0:512], gbuf[:],
                             ALU.mult, reads=[pu_b, gbuf_b], writes=[hT_b])
                for th in range(2):
                    dps = [pb[j] for j in range(4)]
                    first = True
                    for hc in range(nhc):
                        wsl, wsl_b = w3s[hc % 2], w3s_b[hc % 2]
                        src = (w3r[e, hc] if e < E else ws3r[hc])
                        P.load(wsl[:], src, writes=[wsl_b])
                        for i2 in range(2):
                            i = th * 2 + i2
                            for oc in range(2):
                                pd, pd_b = dps[i2 * 2 + oc]
                                P.mm(pd[:, 0:512],
                                     hT[:, hc * 512 + i * 128: hc * 512 + i * 128 + 128],
                                     wsl[:, oc * 512:(oc + 1) * 512],
                                     reads=[hT_b, wsl_b], psum=pd_b,
                                     start=(hc == 0), stop=(hc == nhc - 1))
                    nxt = 1 - cur
                    for i2 in range(2):
                        i = th * 2 + i2
                        for oc in range(2):
                            pd, pd_b = dps[i2 * 2 + oc]
                            if e < E:
                                scal = gsc[:, i * E + e: i * E + e + 1]
                                srd = [pd_b, gsc_b, yacc_b[cur]]
                            else:
                                scal = 1.0 if layer == 0 else 0.0625
                                srd = [pd_b, yacc_b[cur]]
                            P.stt(yacc[nxt][:, i * C + oc * 512: i * C + (oc + 1) * 512],
                                  pd[:, 0:512], scal,
                                  yacc[cur][:, i * C + oc * 512: i * C + (oc + 1) * 512],
                                  ALU.mult, ALU.add, reads=srd, writes=[yacc_b[nxt]])
                cur = 1 - cur
            for i in range(4):
                P.store(y_out[tc * 512 + i * 128: tc * 512 + (i + 1) * 128, :],
                        yacc[cur][:, i * C:(i + 1) * C], reads=[yacc_b[cur]])


def prep_moe_inputs(layer, inp, y_in):
    if y_in is None:
        y_in = np.zeros((1,), np.float32)
    if layer == 0:
        return {
            "y_in": np.ascontiguousarray(y_in),
            "wg": np.ascontiguousarray(inp["L0_ffn_Wg"]),
            "w1": np.ascontiguousarray(inp["L0_ffn_W1"]),
            "w2m": np.ascontiguousarray(inp["L0_ffn_W2"]),
            "w3": np.ascontiguousarray(inp["L0_ffn_W3"]),
            "ws1": np.ascontiguousarray(inp["L0_ffn_Ws1"]),
            "ws2": np.ascontiguousarray(inp["L0_ffn_Ws2"]),
            "ws3": np.ascontiguousarray(inp["L0_ffn_Ws3"]),
        }
    return {
        "y_in": np.ascontiguousarray(y_in),
        "wg": np.ascontiguousarray(inp["L1_ffn_gate_W"].T),
        "w1": np.ascontiguousarray(inp["L1_ffn_w1_e"]),
        "w2m": np.ascontiguousarray(inp["L1_ffn_w3_e"]),
        "w3": np.ascontiguousarray(inp["L1_ffn_w2_e"]),
        "ws1": np.ascontiguousarray(inp["L1_ffn_w1_s"].transpose(1, 0, 2).reshape(C, 4096)),
        "ws2": np.ascontiguousarray(inp["L1_ffn_w3_s"].transpose(1, 0, 2).reshape(C, 4096)),
        "ws3": np.ascontiguousarray(inp["L1_ffn_w2_s"].reshape(4096, C)),
    }


HID = 4096
M_SC = 2.0 / (32768.0 * 1024.0)
NS_A, NS_B, NS_C = 3.4445, -4.775, 2.0315
LR = 0.1


def emit_theta_cast(P, src_f32, dst_bf, st, tag=""):
    stage, stage_b = P.sb("thc_stage" + tag, [128, HID], F32, st)
    stgb, stgb_b = P.sb("thc_stgb" + tag, [128, HID], BF16, st)
    s = src_f32.rearrange("(r p) n -> r p n", p=128)
    d = dst_bf.rearrange("(r p) n -> r p n", p=128)
    for r in range(8):
        P.load(stage[:], s[r], writes=[stage_b])
        P.copy(stgb[:], stage[:], reads=[stage_b], writes=[stgb_b], eng="pool")
        P.store(d[r], stgb[:], reads=[stgb_b])


def tload(P, dst_ap, src_ap, writes, reads=(), q="act"):
    P.emit(q, lambda e, o=dst_ap, i=src_ap: e.dma_start(out=o, in_=i, transpose=True),
           reads=reads, writes=writes, dma="q_" + q)


def emit_ttt_pass(P, cst, pb, sA, d):
    with ExitStack() as st:
        kTsl, kTsl_b = P.sb("tp_kTsl", [128, 8 * 512], BF16, st)
        gsl, gsl_b, vsl, vsl_b = [], [], [], []
        for i in range(2):
            a, ab = P.sb(f"tp_gsl{i}", [128, 8 * 512], BF16, st)
            gsl.append(a)
            gsl_b.append(ab)
            a, ab = P.sb(f"tp_vsl{i}", [128, 8 * 512], BF16, st)
            vsl.append(a)
            vsl_b.append(ab)
        hT, hT_b = P.sb("tp_hT", [128, 32 * 512], BF16, st)
        rT, rT_b = P.sb("tp_rT", [128, 8 * 512], BF16, st)
        s_sb, s_sb_b = P.sb("tp_s", [128, 512], BF16, st)
        u1, u1_b = P.sb("tp_u1", [128, 512], F32, st)
        u2, u2_b = P.sb("tp_u2", [128, 512], F32, st)
        sp_, sp_b = P.sb("tp_sp", [128, 512], F32, st)
        swb, swb_b = P.sb("tp_swb", [128, 512], BF16, st)
        hb, hb_b = P.sb("tp_hb", [128, 512], BF16, st)
        kvb, kvb_b = P.sb("tp_kvb", [128, 512], BF16, st)
        vt, vt_b = P.sb("tp_vt", [128, 1024], F32, st)
        tsc, tsc_b = P.sb("tp_tsc", [128, 512], F32, st)
        rnt, rnt_b = P.sb("tp_rnt", [128, 1024], BF16, st)
        pj, pj_b, swq, swq_b, kvq, kvq_b = [], [], [], [], [], []
        for i in range(2):
            a, ab = P.sb(f"tp_pj{i}", [128, 1024], BF16, st)
            pj.append(a)
            pj_b.append(ab)
            a, ab = P.sb(f"tp_swq{i}", [128, 512], BF16, st)
            swq.append(a)
            swq_b.append(ab)
            a, ab = P.sb(f"tp_kvq{i}", [128, 512], BF16, st)
            kvq.append(a)
            kvq_b.append(ab)
        dkgb, dkgb_b = P.sb("tp_dkgb", [128, 512], BF16, st)
        dkvb, dkvb_b = P.sb("tp_dkvb", [128, 512], BF16, st)
        gate_r = d["gate_bf"].rearrange("(i p) h -> p i h", p=128)
        val_r = d["val_bf"].rearrange("(i p) h -> p i h", p=128)
        projT_r = d["projT_bf"].rearrange("(i p) h -> p i h", p=128)
        for tcx in range(8):
            t0 = tcx * 512
            for ic in range(8):
                tload(P, kTsl[:, ic * 512:(ic + 1) * 512],
                      d["kn"][t0:t0 + 512, ic * 128:(ic + 1) * 128],
                      writes=[kTsl_b], reads=[d["kn_B"]])
            for hc8 in range(8):
                gs, gs_b = gsl[hc8 % 2], gsl_b[hc8 % 2]
                vs, vs_b = vsl[hc8 % 2], vsl_b[hc8 % 2]
                for ic in range(8):
                    P.load(gs[:, ic * 512:(ic + 1) * 512],
                           gate_r[:, ic, hc8 * 512:(hc8 + 1) * 512],
                           writes=[gs_b], reads=[d["gate_bf_B"]])
                    P.load(vs[:, ic * 512:(ic + 1) * 512],
                           val_r[:, ic, hc8 * 512:(hc8 + 1) * 512],
                           writes=[vs_b], reads=[d["val_bf_B"]], q="pool")
                for t2 in range(4):
                    pkg, pkg_b = pb[4]
                    pkv, pkv_b = pb[5]
                    for ic in range(8):
                        P.mm(pkg[:, 0:512],
                             kTsl[:, ic * 512 + t2 * 128: ic * 512 + t2 * 128 + 128],
                             gs[:, ic * 512:(ic + 1) * 512], reads=[kTsl_b, gs_b],
                             psum=pkg_b, start=(ic == 0), stop=(ic == 7))
                    for ic in range(8):
                        P.mm(pkv[:, 0:512],
                             kTsl[:, ic * 512 + t2 * 128: ic * 512 + t2 * 128 + 128],
                             vs[:, ic * 512:(ic + 1) * 512], reads=[kTsl_b, vs_b],
                             psum=pkv_b, start=(ic == 0), stop=(ic == 7))
                    P.act(s_sb[:], pkg[:, 0:512], AF.Sigmoid, reads=[pkg_b],
                          writes=[s_sb_b])
                    P.tt(swb[:], pkg[:, 0:512], s_sb[:], ALU.mult,
                         reads=[pkg_b, s_sb_b], writes=[swb_b])
                    P.tt(hb[:], pkv[:, 0:512], swb[:], ALU.mult,
                         reads=[pkv_b, swb_b], writes=[hb_b])
                    P.ts(u1[:], s_sb[:], -1.0, 1.0, ALU.mult, ALU.add,
                         reads=[s_sb_b], writes=[u1_b])
                    P.tt(u2[:], swb[:], u1[:], ALU.mult, reads=[swb_b, u1_b],
                         writes=[u2_b])
                    P.tt(sp_[:], s_sb[:], u2[:], ALU.add, reads=[s_sb_b, u2_b],
                         writes=[sp_b])
                    P.tt(kvb[:], pkv[:, 0:512], sp_[:], ALU.mult,
                         reads=[pkv_b, sp_b], writes=[kvb_b])
                    rows = slice(t0 + t2 * 128, t0 + (t2 + 1) * 128)
                    hcols = slice(hc8 * 512, (hc8 + 1) * 512)
                    P.store(d["sw_d"][rows, hcols], swb[:], reads=[swb_b],
                            writes=[d["sw_d_B"]])
                    P.store(d["h_d"][rows, hcols], hb[:], reads=[hb_b],
                            writes=[d["h_d_B"]])
                    P.store(d["kvsp_d"][rows, hcols], kvb[:], reads=[kvb_b],
                            writes=[d["kvsp_d_B"]])
            for hc in range(32):
                tload(P, hT[:, hc * 512:(hc + 1) * 512],
                      d["h_d"][t0:t0 + 512, hc * 128:(hc + 1) * 128],
                      writes=[hT_b], reads=[d["h_d_B"]])
            for th in range(2):
                dps = [pb[j] for j in range(4)]
                for hc in range(32):
                    pjs, pjs_b = pj[hc % 2], pj_b[hc % 2]
                    tload(P, pjs[:], d["projT_bf"][:, hc * 128:(hc + 1) * 128],
                          writes=[pjs_b], reads=[d["projT_bf_B"]])
                    for i2 in range(2):
                        t2 = th * 2 + i2
                        for cc in range(2):
                            pd, pd_b = dps[i2 * 2 + cc]
                            P.mm(pd[:, 0:512],
                                 hT[:, hc * 512 + t2 * 128: hc * 512 + t2 * 128 + 128],
                                 pjs[:, cc * 512:(cc + 1) * 512],
                                 reads=[hT_b, pjs_b], psum=pd_b,
                                 start=(hc == 0), stop=(hc == 31))
                for i2 in range(2):
                    t2 = th * 2 + i2
                    rows = slice(t0 + t2 * 128, t0 + (t2 + 1) * 128)
                    P.load(vt[:], d["vn"][rows, :], writes=[vt_b],
                           reads=[d["vn_B"]])
                    for cc in range(2):
                        pd, pd_b = dps[i2 * 2 + cc]
                        P.tt(tsc[:], pd[:, 0:512], vt[:, cc * 512:(cc + 1) * 512],
                             ALU.subtract, reads=[pd_b, vt_b], writes=[tsc_b])
                        P.ts(rnt[:, cc * 512:(cc + 1) * 512], tsc[:], M_SC, None,
                             ALU.mult, None, reads=[tsc_b], writes=[rnt_b])
                    P.store(d["rn_d"][rows, :], rnt[:], reads=[rnt_b],
                            writes=[d["rn_d_B"]])
            for ic in range(8):
                tload(P, rT[:, ic * 512:(ic + 1) * 512],
                      d["rn_d"][t0:t0 + 512, ic * 128:(ic + 1) * 128],
                      writes=[rT_b], reads=[d["rn_d_B"]])
            for hc8 in range(8):
                gs, gs_b = gsl[hc8 % 2], gsl_b[hc8 % 2]
                for ic in range(8):
                    P.load(gs[:, ic * 512:(ic + 1) * 512],
                           projT_r[:, ic, hc8 * 512:(hc8 + 1) * 512],
                           writes=[gs_b], reads=[d["projT_bf_B"]])
                for t2 in range(4):
                    pdh, pdh_b = pb[4]
                    for ic in range(8):
                        P.mm(pdh[:, 0:512],
                             rT[:, ic * 512 + t2 * 128: ic * 512 + t2 * 128 + 128],
                             gs[:, ic * 512:(ic + 1) * 512], reads=[rT_b, gs_b],
                             psum=pdh_b, start=(ic == 0), stop=(ic == 7))
                    rows = slice(t0 + t2 * 128, t0 + (t2 + 1) * 128)
                    hcols = slice(hc8 * 512, (hc8 + 1) * 512)
                    i = t2 % 2
                    P.load(swq[i][:], d["sw_d"][rows, hcols], writes=[swq_b[i]],
                           reads=[d["sw_d_B"]], q="pool")
                    P.load(kvq[i][:], d["kvsp_d"][rows, hcols], writes=[kvq_b[i]],
                           reads=[d["kvsp_d_B"]], q="pool")
                    P.tt(dkgb[:], pdh[:, 0:512], kvq[i][:], ALU.mult,
                         reads=[pdh_b, kvq_b[i]], writes=[dkgb_b])
                    P.tt(dkvb[:], pdh[:, 0:512], swq[i][:], ALU.mult,
                         reads=[pdh_b, swq_b[i]], writes=[dkvb_b])
                    P.store(d["dkg_d"][rows, hcols], dkgb[:], reads=[dkgb_b],
                            writes=[d["dkg_d_B"]])
                    P.store(d["dkv_d"][rows, hcols], dkvb[:], reads=[dkvb_b],
                            writes=[d["dkv_d_B"]])
        P.barrier()
    with ExitStack() as st:
        rhs_sb, rhs_sb_b = P.sb("tp_rhs", [128, 32 * 1024], BF16, st)
        blk_, blk_b = [], []
        for i in range(2):
            a, ab = P.sb(f"tp_blk{i}", [128, 512], BF16, st)
            blk_.append(a)
            blk_b.append(ab)
        gout, gout_b = P.sb("tp_gout", [128, 512], F32, st)
        jobs = [("dkg_d", "pg_o", "kn"), ("dkv_d", "pv_o", "kn"),
                ("h_d", "pp_o", "rn_d")]
        cur_rhs = None
        for (src_, dst, rhs) in jobs:
            if rhs != cur_rhs:
                P.barrier()
                for tt in range(32):
                    P.load(rhs_sb[:, tt * 1024:(tt + 1) * 1024],
                           d[rhs][tt * 128:(tt + 1) * 128, :],
                           writes=[rhs_sb_b], reads=[d[rhs + "_B"]])
                cur_rhs = rhs
            for hg in range(8):
                for cc in range(2):
                    for tt in range(32):
                        i = tt % 2
                        P.load(blk_[i][:], d[src_][tt * 128:(tt + 1) * 128,
                                                   hg * 512:(hg + 1) * 512],
                               writes=[blk_b[i]], reads=[d[src_ + "_B"]])
                        for hs in range(4):
                            pd, pd_b = pb[hs]
                            P.mm(pd[:, 0:512],
                                 blk_[i][:, hs * 128:(hs + 1) * 128],
                                 rhs_sb[:, tt * 1024 + cc * 512: tt * 1024 + (cc + 1) * 512],
                                 reads=[blk_b[i], rhs_sb_b], psum=pd_b,
                                 start=(tt == 0), stop=(tt == 31))
                    for hs in range(4):
                        pd, pd_b = pb[hs]
                        P.copy(gout[:], pd[:, 0:512], reads=[pd_b],
                               writes=[gout_b], eng="dve")
                        P.store(d[dst][(hg * 4 + hs) * 128:(hg * 4 + hs + 1) * 128,
                                       cc * 512:(cc + 1) * 512],
                                gout[:], reads=[gout_b], writes=[d[dst + "_B"]])
        P.barrier()


def emit_ns_update(P, cst, pb, ptb, sA, mats, dbg=None):
    onesf = cst["ones_f"]
    with ExitStack() as st:
        X_sb, X_b = P.sb("ns_X", [128, 8 * HID], BF16, st)
        XT_sb, XT_b = P.sb("ns_XT", [128, 32 * C], BF16, st)
        A_sb, A_b = P.sb("ns_A", [128, 8 * 1024], BF16, st)
        S_sb, S_b = P.sb("ns_S", [128, 8 * 1024], BF16, st)
        Y_sb, Y_b = A_sb, A_b
        stt_, stt_b = P.sb("ns_stats", [128, 8], F32, st)
        st2, st2_b = P.sb("ns_st2", [1, 8], F32, st)
        rnb, rnb_b = P.sb("ns_rnb", [128, 1], F32, st)
        tho, tho_b = P.sb("ns_tho", [128, 512], F32, st)
        thn, thn_b = P.sb("ns_thn", [128, 512], F32, st)
        thnb, thnb_b = P.sb("ns_thnb", [128, 512], BF16, st)
        pt, pt_b = ptb
        for mi, m in enumerate(mats):
            with ExitStack() as st1:
                g8, g8_b = P.sb(f"ns_g8_{mi}", [128, 2 * 1024], F32, st1)
                gsum, gsum_b = P.sb(f"ns_gsum_{mi}", [128, 1024], F32, st1)
                x0t_t, x0t_t_b = P.sb(f"ns_x0t_{mi}", [128, 1024], BF16, st1)
                pall_r = m["p_all"]
                g8sl = [g8_b, Buf(f"g8b{mi}")]
                for pc in range(32):
                    for k in range(8):
                        sl = g8[:, (k % 2) * 1024:(k % 2 + 1) * 1024]
                        bb = g8sl[k % 2]
                        P.load(sl, pall_r[k, pc * 128:(pc + 1) * 128, :],
                               writes=[bb], reads=[m["p_all_B"]],
                               q=("sp" if k % 2 == 0 else "pool"))
                        if k == 0:
                            P.copy(gsum[:], sl, reads=[bb],
                                   writes=[gsum_b], eng="dve")
                        else:
                            P.tt(gsum[:], gsum[:], sl, ALU.add,
                                 reads=[bb, gsum_b], writes=[gsum_b])
                    P.act(g8[:, 0:1024], gsum[:], AF.Square, reads=[gsum_b],
                          writes=[g8_b])
                    P.red(stt_[:, 0:1], g8[:, 0:1024], ALU.add, reads=[g8_b],
                          writes=[stt_b])
                    if pc == 0:
                        P.copy(stt_[:, 1:2], stt_[:, 0:1], reads=[stt_b],
                               writes=[stt_b])
                    else:
                        P.tt(stt_[:, 1:2], stt_[:, 1:2], stt_[:, 0:1], ALU.add,
                             reads=[stt_b], writes=[stt_b])
                    P.store(m["Gsum_d"][pc * 128:(pc + 1) * 128, :], gsum[:],
                            reads=[gsum_b], writes=[m["Gsum_d_B"]])
                pd, pd_b = pb[0]
                P.mm(pd[0:1, 0:1], stt_[:, 1:2], onesf[0][:, 0:1],
                     reads=[stt_b, onesf[1]], psum=pd_b, start=True, stop=True)
                P.copy(st2[0:1, 0:1], pd[0:1, 0:1], reads=[pd_b], writes=[st2_b],
                       eng="dve")
                P.act(st2[0:1, 1:2], st2[0:1, 0:1], AF.Sqrt, reads=[st2_b],
                      writes=[st2_b])
                P.ts(st2[0:1, 2:3], st2[0:1, 1:2], 1e-7, None, ALU.add, None,
                     reads=[st2_b], writes=[st2_b])
                P.recip(st2[0:1, 3:4], st2[0:1, 2:3], reads=[st2_b],
                        writes=[st2_b])
                P.mm(pd[:, 0:1], onesf[0][0:1, :], st2[0:1, 3:4],
                     reads=[st2_b, onesf[1]], psum=pd_b, start=True, stop=True)
                P.copy(rnb[:], pd[:, 0:1], reads=[pd_b], writes=[rnb_b], eng="dve")
                for pc in range(32):
                    P.load(gsum[:], m["Gsum_d"][pc * 128:(pc + 1) * 128, :],
                           writes=[gsum_b], reads=[m["Gsum_d_B"]])
                    P.ts(x0t_t[:], gsum[:], rnb[:, 0:1], None, ALU.mult, None,
                         reads=[gsum_b, rnb_b], writes=[x0t_t_b])
                    P.store(m["X0T_d"][pc * 128:(pc + 1) * 128, :], x0t_t[:],
                            reads=[x0t_t_b], writes=[m["X0T_d_B"]])
                P.barrier()
            for ic in range(8):
                tload(P, X_sb[:, ic * HID:(ic + 1) * HID],
                      m["X0T_d"][:, ic * 128:(ic + 1) * 128], writes=[X_b],
                      reads=[m["X0T_d_B"]])
            for hc in range(32):
                P.load(XT_sb[:, hc * 1024:(hc + 1) * 1024],
                       m["X0T_d"][hc * 128:(hc + 1) * 128, :], writes=[XT_b],
                       reads=[m["X0T_d_B"]])
            for it in range(5):
                for cmb in range(16):
                    ic, jc = cmb // 2, cmb % 2
                    pd, pd_b = pb[cmb % 4]
                    for hc in range(32):
                        P.mm(pd[:, 0:512],
                             XT_sb[:, hc * 1024 + ic * 128: hc * 1024 + ic * 128 + 128],
                             XT_sb[:, hc * 1024 + jc * 512: hc * 1024 + (jc + 1) * 512],
                             reads=[XT_b], psum=pd_b,
                             start=(hc == 0), stop=(hc == 31))
                    P.copy(A_sb[:, ic * 1024 + jc * 512: ic * 1024 + (jc + 1) * 512],
                           pd[:, 0:512], reads=[pd_b], writes=[A_b], eng="dve")
                for cmb in range(16):
                    ic, jc = cmb // 2, cmb % 2
                    pd, pd_b = pb[cmb % 4]
                    for kc in range(8):
                        P.mm(pd[:, 0:512],
                             A_sb[:, kc * 1024 + ic * 128: kc * 1024 + ic * 128 + 128],
                             A_sb[:, kc * 1024 + jc * 512: kc * 1024 + (jc + 1) * 512],
                             reads=[A_b], psum=pd_b, start=(kc == 0), stop=(kc == 7))
                    P.copy(S_sb[:, ic * 1024 + jc * 512: ic * 1024 + (jc + 1) * 512],
                           pd[:, 0:512], reads=[pd_b], writes=[S_b], eng="dve")
                for ic in range(8):
                    sl = slice(ic * 1024, (ic + 1) * 1024)
                    P.ts(A_sb[:, sl], A_sb[:, sl], NS_B, None, ALU.mult, None,
                         reads=[A_b], writes=[A_b])
                    P.stt(A_sb[:, sl], S_sb[:, sl], NS_C, A_sb[:, sl],
                          ALU.mult, ALU.add, reads=[S_b, A_b], writes=[A_b])
                for hc in range(32):
                    for jc in range(2):
                        pd, pd_b = pb[4 + (hc * 2 + jc) % 2]
                        for ic in range(8):
                            P.mm(pd[:, 0:512],
                                 X_sb[:, ic * HID + hc * 128: ic * HID + hc * 128 + 128],
                                 Y_sb[:, ic * 1024 + jc * 512: ic * 1024 + (jc + 1) * 512],
                                 reads=[X_b, Y_b], psum=pd_b,
                                 start=(ic == 0), stop=(ic == 7))
                        sl = slice(hc * 1024 + jc * 512, hc * 1024 + (jc + 1) * 512)
                        P.stt(XT_sb[:, sl], XT_sb[:, sl], NS_A, pd[:, 0:512],
                              ALU.mult, ALU.add, reads=[pd_b, XT_b], writes=[XT_b])
                for ic in range(8):
                    for hg in range(4):
                        for q_ in range(8):
                            hc = hg * 8 + q_
                            P.trans(pt[:, q_ * 128:(q_ + 1) * 128],
                                    XT_sb[:, hc * 1024 + ic * 128: hc * 1024 + ic * 128 + 128],
                                    cst["ident_b"][0][:],
                                    reads=[XT_b, cst["ident_b"][1]], psum=pt_b)
                        P.copy(X_sb[:, ic * HID + hg * 1024: ic * HID + (hg + 1) * 1024],
                               pt[:, 0:1024], reads=[pt_b], writes=[X_b], eng="dve")
            tho_r = m["th_old"].rearrange("(i p) h -> p i h", p=128)
            thn_r = m["th_new"].rearrange("(i p) h -> p i h", p=128)
            thnb_r = m["th_new_bf"].rearrange("(i p) h -> p i h", p=128)
            for ic in range(8):
                for hb in range(8):
                    P.load(tho[:], tho_r[:, ic, hb * 512:(hb + 1) * 512],
                           writes=[tho_b], reads=[m["th_old_B"]])
                    P.stt(thn[:], X_sb[:, ic * HID + hb * 512: ic * HID + (hb + 1) * 512],
                          -LR, tho[:], ALU.mult, ALU.add,
                          reads=[X_b, tho_b], writes=[thn_b])
                    P.copy(thnb[:], thn[:], reads=[thn_b], writes=[thnb_b],
                           eng="pool")
                    P.store(thn_r[:, ic, hb * 512:(hb + 1) * 512], thn[:],
                            reads=[thn_b], writes=[m["th_new_B"]])
                    P.store(thnb_r[:, ic, hb * 512:(hb + 1) * 512], thnb[:],
                            reads=[thnb_b], writes=[m["th_new_bf_B"]])
        P.barrier()


def ttt_drams(nc, kind_partials="ExternalOutput"):
    d = {}

    def mk(name, shape, dt, kind="Internal"):
        ap = nc.dram_tensor(name, list(shape), dt, kind=kind).ap()
        d[name] = ap
        d[name + "_B"] = Buf(name)
        return ap
    mk("gate_bf", (C, HID), BF16)
    mk("val_bf", (C, HID), BF16)
    mk("projT_bf", (C, HID), BF16)
    mk("h_d", (T, HID), BF16)
    mk("sw_d", (T, HID), BF16)
    mk("kvsp_d", (T, HID), BF16)
    mk("dkg_d", (T, HID), BF16)
    mk("dkv_d", (T, HID), BF16)
    mk("rn_d", (T, C), BF16)
    mk("pg_o", (HID, C), F32, kind_partials)
    mk("pv_o", (HID, C), F32, kind_partials)
    mk("pp_o", (HID, C), F32, kind_partials)
    return d


def emit_l1_c1(nc, P, cst, pb, ptb, io):
    y2 = io["y2"]
    qkvw_d = io["qkvw"]
    qkvb_d = io["qkvb"]
    qsc_d = io["qsc"]
    qbi_d = io["qbi"]
    ksc_d = io["ksc"]
    kbi_d = io["kbi"]
    g0_d = io["gate0"]
    v0_d = io["val0"]
    p0_d = io["projT0"]
    qn_o = io["qn_o"]
    kn_o = io["kn_o"]
    vn_o = io["vn_o"]
    with ExitStack() as sA:
        d = ttt_drams(nc)
        d["kn"] = kn_o
        d["kn_B"] = Buf("kn_o")
        d["vn"] = vn_o
        d["vn_B"] = Buf("vn_o")
        with ExitStack() as sW:
            emit_theta_cast(P, g0_d, d["gate_bf"], sW, "g")
            emit_theta_cast(P, v0_d, d["val_bf"], sW, "v")
            emit_theta_cast(P, p0_d, d["projT_bf"], sW, "p")
            P.barrier()
        for nm in ("gate_bf", "val_bf", "projT_bf"):
            d[nm + "_B"].w["q_pool"] = P.cnt["q_pool"]
        with ExitStack() as s1:
            xnT, xnT_b = P.sb("c1_xnT", [128, 8 * T], BF16, s1)
            with ExitStack() as s1a:
                emit_ln_to_xnT(P, cst, ptb, y2, xnT, xnT_b, s1a, prefix="c1ln")
                P.barrier()
            with ExitStack() as s1b:
                wstage, wstage_b = P.sb("c1_wst", [128, 3 * C], F32, s1b)
                wbf, wbf_b = P.sb("c1_wbf", [128, 8 * 3 * C], BF16, s1b)
                load_w_bf16(P, qkvw_d, wbf, wbf_b, wstage, wstage_b, C, 3 * C,
                            halves=8)
                qkvb, qkvb_b = P.sb("c1_qkvb", [128, 3 * C], F32, s1b)
                qsc, qsc_b = P.sb("c1_qsc", [128, C], F32, s1b)
                qbi, qbi_b = P.sb("c1_qbi", [128, C], F32, s1b)
                ksc, ksc_b = P.sb("c1_ksc", [128, C], F32, s1b)
                kbi, kbi_b = P.sb("c1_kbi", [128, C], F32, s1b)
                P.load(qkvb[:], qkvb_d, writes=[qkvb_b])
                P.load(qsc[:], qsc_d, writes=[qsc_b])
                P.load(qbi[:], qbi_d, writes=[qbi_b])
                P.load(ksc[:], ksc_d, writes=[ksc_b])
                P.load(kbi[:], kbi_d, writes=[kbi_b])
                qkv_sb, qkv_sb_b = P.sb("c1_qkv", [128, 3 * C], F32, s1b)
                sq2, sq2_b = P.sb("c1_sq2", [128, C], F32, s1b)
                nrm, nrm_b = P.sb("c1_nrm", [128, 4 * 16], F32, s1b)
                on_, on_b = P.sb("c1_on", [128, C], F32, s1b)
                onb, onb_b = P.sb("c1_onb", [128, C], BF16, s1b)
                for tt in range(32):
                    for oc in range(6):
                        pd, pd_b = pb[oc % 2]
                        for ic in range(8):
                            P.mm(pd[:, 0:512],
                                 xnT[:, ic * T + tt * 128: ic * T + tt * 128 + 128],
                                 wbf[:, ic * 3 * C + oc * 512: ic * 3 * C + (oc + 1) * 512],
                                 reads=[xnT_b, wbf_b], psum=pd_b,
                                 start=(ic == 0), stop=(ic == 7))
                            pass
                        P.tt(qkv_sb[:, oc * 512:(oc + 1) * 512], pd[:, 0:512],
                             qkvb[:, oc * 512:(oc + 1) * 512], ALU.add,
                             reads=[pd_b, qkvb_b], writes=[qkv_sb_b])
                    for part, sc, scb, bi, bib, out_d in (
                            (0, qsc, qsc_b, qbi, qbi_b, qn_o),
                            (1, ksc, ksc_b, kbi, kbi_b, kn_o)):
                        off = part * C
                        P.act(sq2[:], qkv_sb[:, off:off + C], AF.Square,
                              reads=[qkv_sb_b], writes=[sq2_b])
                        P.red(nrm[:, 0:16], sq2[:].rearrange("p (h e) -> p h e", e=64),
                              ALU.add, reads=[sq2_b], writes=[nrm_b])
                        P.act(nrm[:, 16:32], nrm[:, 0:16], AF.Sqrt, reads=[nrm_b],
                              writes=[nrm_b])
                        P.ts(nrm[:, 32:48], nrm[:, 16:32], 1e-6, None, ALU.add, None,
                             reads=[nrm_b], writes=[nrm_b])
                        P.recip(nrm[:, 48:64], nrm[:, 32:48], reads=[nrm_b],
                                writes=[nrm_b])
                        for h in range(16):
                            P.ts(on_[:, h * 64:(h + 1) * 64],
                                 qkv_sb[:, off + h * 64: off + (h + 1) * 64],
                                 nrm[:, 48 + h:49 + h], None, ALU.mult, None,
                                 reads=[qkv_sb_b, nrm_b], writes=[on_b])
                        P.tt(on_[:], on_[:], sc[:], ALU.mult, reads=[on_b, scb],
                             writes=[on_b])
                        P.tt(onb[:], on_[:], bi[:], ALU.add, reads=[on_b, bib],
                             writes=[onb_b])
                        bufkey = d["kn_B"] if part == 1 else Buf("qn")
                        P.store(out_d[tt * 128:(tt + 1) * 128, :], onb[:],
                                reads=[onb_b],
                                writes=[bufkey])
                    P.store(vn_o[tt * 128:(tt + 1) * 128, :],
                            qkv_sb[:, 2 * C:3 * C], reads=[qkv_sb_b],
                            writes=[d["vn_B"]])
                P.barrier()
            P.barrier()
        emit_ttt_pass(P, cst, pb, sA, d)


def build_l1_attn_c2():
    nc = bass.Bass("TRN2", target_bir_lowering=False)
    stacks = {}
    for nm in ("pg_all", "pv_all", "pp_all"):
        stacks[nm] = nc.dram_tensor(nm, [8, HID, C], F32, kind="ExternalInput").ap()
    g0_d = nc.dram_tensor("gate0", [C, HID], F32, kind="ExternalInput").ap()
    v0_d = nc.dram_tensor("val0", [C, HID], F32, kind="ExternalInput").ap()
    p0_d = nc.dram_tensor("projT0", [C, HID], F32, kind="ExternalInput").ap()
    kn_i = nc.dram_tensor("kn_i", [T, C], BF16, kind="ExternalInput").ap()
    vn_i = nc.dram_tensor("vn_i", [T, C], F32, kind="ExternalInput").ap()
    g1_d = nc.dram_tensor("gate1", [C, HID], F32, kind="ExternalOutput").ap()
    v1_d = nc.dram_tensor("val1", [C, HID], F32, kind="ExternalOutput").ap()
    p1_d = nc.dram_tensor("projT1", [C, HID], F32, kind="ExternalOutput").ap()
    with ExitStack() as sA:
        P = Prog(nc, sA)
        cst = alloc_consts(nc, P, sA)
        pb = [P.ps(f"pb{i}", [128, 512], F32, sA) for i in range(7)]
        ptb = P.ps("ptb", [128, 1024], BF16, sA)
        d = ttt_drams(nc)
        d["kn"] = kn_i
        d["kn_B"] = Buf("kn_i")
        d["vn"] = vn_i
        d["vn_B"] = Buf("vn_i")
        gsum_d = nc.dram_tensor("gsum_d", [HID, C], F32, kind="Internal").ap()
        x0t_d = nc.dram_tensor("x0t_d", [HID, C], BF16, kind="Internal").ap()
        mats = []
        for (pal, tho, thn, thnbf) in (("pg_all", g0_d, g1_d, "gate_bf"),
                                       ("pv_all", v0_d, v1_d, "val_bf"),
                                       ("pp_all", p0_d, p1_d, "projT_bf")):
            mats.append(dict(p_all=stacks[pal], p_all_B=Buf(pal),
                             th_old=tho, th_old_B=Buf("tho"),
                             th_new=thn, th_new_B=Buf("thn"),
                             th_new_bf=d[thnbf], th_new_bf_B=d[thnbf + "_B"],
                             Gsum_d=gsum_d, Gsum_d_B=Buf("gsum"),
                             X0T_d=x0t_d, X0T_d_B=Buf("x0t")))
        emit_ns_update(P, cst, pb, ptb, sA, mats)
        emit_ttt_pass(P, cst, pb, sA, d)
        P.drain()
        P.finalize()
    return nc


def emit_l1_c3(nc, P, cst, pb, ptb, io):
    stacks = {nm: io[nm] for nm in ("pg_all", "pv_all", "pp_all")}
    g1_d = io["gate1"]
    v1_d = io["val1"]
    p1_d = io["projT1"]
    qn_i = io["qn_i"]
    y2 = io["y2"]
    pw_d = io["pw"]
    pjb_d = io["pjb"]
    y3_d = io["y3"]
    with ExitStack() as sA:
        gate_bf = nc.dram_tensor("gate_bf", [C, HID], BF16, kind="Internal").ap()
        val_bf = nc.dram_tensor("val_bf", [C, HID], BF16, kind="Internal").ap()
        projT_bf = nc.dram_tensor("projT_bf", [C, HID], BF16, kind="Internal").ap()
        bufs = {nm: Buf(nm) for nm in ("gate_bf", "val_bf", "projT_bf")}
        gsum_d = nc.dram_tensor("gsum_d", [HID, C], F32, kind="Internal").ap()
        x0t_d = nc.dram_tensor("x0t_d", [HID, C], BF16, kind="Internal").ap()
        g2_d = nc.dram_tensor("gate2", [C, HID], F32, kind="Internal").ap()
        v2_d = nc.dram_tensor("val2", [C, HID], F32, kind="Internal").ap()
        p2_d = nc.dram_tensor("projT2", [C, HID], F32, kind="Internal").ap()
        mats = []
        for (pal, tho, thn, thnbf) in (("pg_all", g1_d, g2_d, "gate_bf"),
                                       ("pv_all", v1_d, v2_d, "val_bf"),
                                       ("pp_all", p1_d, p2_d, "projT_bf")):
            mats.append(dict(p_all=stacks[pal], p_all_B=Buf(pal),
                             th_old=tho, th_old_B=Buf("tho"),
                             th_new=thn, th_new_B=Buf("thn"),
                             th_new_bf={"gate_bf": gate_bf, "val_bf": val_bf,
                                        "projT_bf": projT_bf}[thnbf],
                             th_new_bf_B=bufs[thnbf],
                             Gsum_d=gsum_d, Gsum_d_B=Buf("gsum"),
                             X0T_d=x0t_d, X0T_d_B=Buf("x0t")))
        emit_ns_update(P, cst, pb, ptb, sA, mats)
        with ExitStack() as st:
            pwst, pwst_b = P.sb("c3_pwst", [128, 4 * C], F32, st)
            pwbf, pwbf_b = P.sb("c3_pwbf", [128, 8 * C], BF16, st)
            load_w_bf16(P, pw_d, pwbf, pwbf_b, pwst, pwst_b, C, C)
            pjb, pjb_b = P.sb("c3_pjb", [128, 8], F32, st)
            P.load(pjb[:], pjb_d, writes=[pjb_b])
            qTsl, qTsl_b = P.sb("c3_qTsl", [128, 8 * 512], BF16, st)
            gsl, gsl_b, vsl, vsl_b = [], [], [], []
            for i in range(2):
                a, ab = P.sb(f"c3_gsl{i}", [128, 8 * 512], BF16, st)
                gsl.append(a)
                gsl_b.append(ab)
                a, ab = P.sb(f"c3_vsl{i}", [128, 8 * 512], BF16, st)
                vsl.append(a)
                vsl_b.append(ab)
            hqT, hqT_b = P.sb("c3_hqT", [128, 32 * 512], BF16, st)
            s_sb, s_sb_b = P.sb("c3_s", [128, 512], BF16, st)
            tw, tw_b = P.sb("c3_tw", [128, 512], F32, st)
            pj2, pj2_b = [], []
            for i in range(2):
                a, ab = P.sb(f"c3_pj2{i}", [128, 1024], BF16, st)
                pj2.append(a)
                pj2_b.append(ab)
            o1T, o1T_b = P.sb("c3_o1T", [128, 8 * 512], BF16, st)
            o2T, o2T_b = P.sb("c3_o2T", [128, 8 * 512], F32, st)
            y2t, y2t_b = P.sb("c3_y2t", [128, C], F32, st)
            y3t, y3t_b = P.sb("c3_y3t", [128, C], F32, st)
            gate_r = gate_bf.rearrange("(i p) h -> p i h", p=128)
            val_r = val_bf.rearrange("(i p) h -> p i h", p=128)
            for tcx in range(8):
                t0 = tcx * 512
                for ic in range(8):
                    tload(P, qTsl[:, ic * 512:(ic + 1) * 512],
                          qn_i[t0:t0 + 512, ic * 128:(ic + 1) * 128],
                          writes=[qTsl_b])
                for hc8 in range(8):
                    gs, gs_b = gsl[hc8 % 2], gsl_b[hc8 % 2]
                    vs, vs_b = vsl[hc8 % 2], vsl_b[hc8 % 2]
                    for ic in range(8):
                        P.load(gs[:, ic * 512:(ic + 1) * 512],
                               gate_r[:, ic, hc8 * 512:(hc8 + 1) * 512],
                               writes=[gs_b], reads=[bufs["gate_bf"]])
                        P.load(vs[:, ic * 512:(ic + 1) * 512],
                               val_r[:, ic, hc8 * 512:(hc8 + 1) * 512],
                               writes=[vs_b], reads=[bufs["val_bf"]], q="pool")
                    for hs in range(4):
                        pqg, pqg_b = pb[4]
                        pqv, pqv_b = pb[5]
                        for ic in range(8):
                            P.mm(pqg[:, 0:512],
                                 gs[:, ic * 512 + hs * 128: ic * 512 + hs * 128 + 128],
                                 qTsl[:, ic * 512:(ic + 1) * 512],
                                 reads=[gs_b, qTsl_b], psum=pqg_b,
                                 start=(ic == 0), stop=(ic == 7))
                        for ic in range(8):
                            P.mm(pqv[:, 0:512],
                                 vs[:, ic * 512 + hs * 128: ic * 512 + hs * 128 + 128],
                                 qTsl[:, ic * 512:(ic + 1) * 512],
                                 reads=[vs_b, qTsl_b], psum=pqv_b,
                                 start=(ic == 0), stop=(ic == 7))
                        P.act(s_sb[:], pqg[:, 0:512], AF.Sigmoid, reads=[pqg_b],
                              writes=[s_sb_b])
                        P.tt(tw[:], pqg[:, 0:512], s_sb[:], ALU.mult,
                             reads=[pqg_b, s_sb_b], writes=[tw_b])
                        P.tt(hqT[:, (hc8 * 4 + hs) * 512:(hc8 * 4 + hs + 1) * 512],
                             pqv[:, 0:512], tw[:], ALU.mult,
                             reads=[pqv_b, tw_b], writes=[hqT_b])
                for ch in range(2):
                    for hc in range(32):
                        pjs, pjs_b = pj2[hc % 2], pj2_b[hc % 2]
                        tload(P, pjs[:], projT_bf[:, hc * 128:(hc + 1) * 128],
                              writes=[pjs_b], reads=[bufs["projT_bf"]])
                        for ccq in range(4):
                            cc = ch * 4 + ccq
                            pd, pd_b = pb[ccq]
                            P.mm(pd[:, 0:512], pjs[:, cc * 128:(cc + 1) * 128],
                                 hqT[:, hc * 512:(hc + 1) * 512],
                                 reads=[pjs_b, hqT_b], psum=pd_b,
                                 start=(hc == 0), stop=(hc == 31))
                    for ccq in range(4):
                        cc = ch * 4 + ccq
                        pd, pd_b = pb[ccq]
                        P.copy(o1T[:, cc * 512:(cc + 1) * 512], pd[:, 0:512],
                               reads=[pd_b], writes=[o1T_b], eng="dve")
                for cc2 in range(8):
                    pd, pd_b = pb[4 + cc2 % 2]
                    for cc in range(8):
                        P.mm(pd[:, 0:512],
                             pwbf[:, cc * C + cc2 * 128: cc * C + cc2 * 128 + 128],
                             o1T[:, cc * 512:(cc + 1) * 512],
                             reads=[pwbf_b, o1T_b], psum=pd_b,
                             start=(cc == 0), stop=(cc == 7))
                    P.act(o2T[:, cc2 * 512:(cc2 + 1) * 512], pd[:, 0:512],
                          AF.Identity, reads=[pd_b, pjb_b], writes=[o2T_b],
                          bias=pjb[:, cc2:cc2 + 1])
                for t2 in range(4):
                    rows = slice(t0 + t2 * 128, t0 + (t2 + 1) * 128)
                    P.load(y2t[:], y2[rows, :], writes=[y2t_b])
                    for half in range(2):
                        pd, pd_b = pb[5 + half]
                        for q_ in range(4):
                            cc2 = half * 4 + q_
                            P.trans(pd[:, q_ * 128:(q_ + 1) * 128],
                                    o2T[:, cc2 * 512 + t2 * 128: cc2 * 512 + t2 * 128 + 128],
                                    cst["ident_f"][0][:],
                                    reads=[o2T_b, cst["ident_f"][1]], psum=pd_b)
                        P.tt(y3t[:, half * 512:(half + 1) * 512], pd[:, 0:512],
                             y2t[:, half * 512:(half + 1) * 512], ALU.add,
                             reads=[pd_b, y2t_b], writes=[y3t_b])
                    P.store(y3_d[rows, :], y3t[:], reads=[y3t_b])
            P.barrier()


def prep_c1_inputs(inp, b):
    bt = lambda vv, n: np.ascontiguousarray(
        np.broadcast_to(np.tile(vv, n), (128, len(vv) * n)).astype(np.float32))
    return dict(
        y2=None,
        qkvw=np.ascontiguousarray(inp["L1_attn_qkv_w"]),
        qkvb=np.ascontiguousarray(np.broadcast_to(inp["L1_attn_qkv_b"], (128, 3 * C))),
        qsc=bt(inp["L1_attn_qn_scale"], 16), qbi=bt(inp["L1_attn_qn_bias"], 16),
        ksc=bt(inp["L1_attn_kn_scale"], 16), kbi=bt(inp["L1_attn_kn_bias"], 16),
        gate0=np.ascontiguousarray(inp["L1_attn_ttt_gate"]),
        val0=np.ascontiguousarray(inp["L1_attn_ttt_val"]),
        projT0=np.ascontiguousarray(inp["L1_attn_ttt_proj"].T),
    )


def build_stage1():
    nc = bass.Bass("TRN2", target_bir_lowering=False)
    io = {}
    io["x"] = nc.dram_tensor("x", [T, C], F32, kind="ExternalInput").ap()
    for k in ("q", "k", "v", "o"):
        io[k + "wT"] = nc.dram_tensor(k + "wT", [C, C], F32,
                                      kind="ExternalInput").ap()
    io["qb"] = nc.dram_tensor("qb", [128, 8], F32, kind="ExternalInput").ap()
    io["kb"] = nc.dram_tensor("kb", [128, 8], F32, kind="ExternalInput").ap()
    io["convw"] = nc.dram_tensor("convw", [128, 24], F32, kind="ExternalInput").ap()
    io["convb"] = nc.dram_tensor("convb", [128, 8], F32, kind="ExternalInput").ap()
    io["vbt"] = nc.dram_tensor("vbt", [128, C], F32, kind="ExternalInput").ap()
    io["obt"] = nc.dram_tensor("obt", [128, C], F32, kind="ExternalInput").ap()
    io["y1"] = nc.dram_tensor("y1", [T, C], F32, kind="Internal").ap()
    E, H, HS = 8, 1408, 2816
    mio = dict(
        y_in=io["y1"],
        wg=nc.dram_tensor("wg", [C, E], F32, kind="ExternalInput").ap(),
        w1=nc.dram_tensor("w1", [E, C, H], F32, kind="ExternalInput").ap(),
        w2m=nc.dram_tensor("w2m", [E, C, H], F32, kind="ExternalInput").ap(),
        w3=nc.dram_tensor("w3", [E, H, C], F32, kind="ExternalInput").ap(),
        ws1=nc.dram_tensor("ws1", [C, HS], F32, kind="ExternalInput").ap(),
        ws2=nc.dram_tensor("ws2", [C, HS], F32, kind="ExternalInput").ap(),
        ws3=nc.dram_tensor("ws3", [HS, C], F32, kind="ExternalInput").ap(),
        y_out=nc.dram_tensor("y2_out", [T, C], F32, kind="ExternalOutput").ap(),
    )
    cio = dict(
        y2=mio["y_out"],
        qkvw=nc.dram_tensor("qkvw", [C, 3 * C], F32, kind="ExternalInput").ap(),
        qkvb=nc.dram_tensor("qkvb", [128, 3 * C], F32, kind="ExternalInput").ap(),
        qsc=nc.dram_tensor("qsc", [128, C], F32, kind="ExternalInput").ap(),
        qbi=nc.dram_tensor("qbi", [128, C], F32, kind="ExternalInput").ap(),
        ksc=nc.dram_tensor("ksc", [128, C], F32, kind="ExternalInput").ap(),
        kbi=nc.dram_tensor("kbi", [128, C], F32, kind="ExternalInput").ap(),
        gate0=nc.dram_tensor("gate0", [C, HID], F32, kind="ExternalInput").ap(),
        val0=nc.dram_tensor("val0", [C, HID], F32, kind="ExternalInput").ap(),
        projT0=nc.dram_tensor("projT0", [C, HID], F32, kind="ExternalInput").ap(),
        qn_o=nc.dram_tensor("qn_o", [T, C], BF16, kind="ExternalOutput").ap(),
        kn_o=nc.dram_tensor("kn_o", [T, C], BF16, kind="ExternalOutput").ap(),
        vn_o=nc.dram_tensor("vn_o", [T, C], F32, kind="ExternalOutput").ap(),
    )
    with ExitStack() as sT:
        P = Prog(nc, sT)
        cst = alloc_consts(nc, P, sT)
        pb = [P.ps(f"pb{i}", [128, 512], F32, sT) for i in range(7)]
        ptb = P.ps("ptb", [128, 1024], BF16, sT)
        emit_l0_attn(nc, P, cst, pb, ptb, io)
        P.barrier()
        emit_moe_ffn(nc, P, cst, pb, ptb, 0, mio)
        P.barrier()
        emit_l1_c1(nc, P, cst, pb, ptb, cio)
        P.drain()
        P.finalize()
    return nc


def build_stage3():
    nc = bass.Bass("TRN2", target_bir_lowering=False)
    cio = {}
    for nm in ("pg_all", "pv_all", "pp_all"):
        cio[nm] = nc.dram_tensor(nm, [8, HID, C], F32, kind="ExternalInput").ap()
    cio["gate1"] = nc.dram_tensor("gate1", [C, HID], F32, kind="ExternalInput").ap()
    cio["val1"] = nc.dram_tensor("val1", [C, HID], F32, kind="ExternalInput").ap()
    cio["projT1"] = nc.dram_tensor("projT1", [C, HID], F32, kind="ExternalInput").ap()
    cio["qn_i"] = nc.dram_tensor("qn_i", [T, C], BF16, kind="ExternalInput").ap()
    cio["y2"] = nc.dram_tensor("y2", [T, C], F32, kind="ExternalInput").ap()
    cio["pw"] = nc.dram_tensor("pw", [C, C], F32, kind="ExternalInput").ap()
    cio["pjb"] = nc.dram_tensor("pjb", [128, 8], F32, kind="ExternalInput").ap()
    cio["y3"] = nc.dram_tensor("y3", [T, C], F32, kind="Internal").ap()
    E, H, HS = 8, 4096, 4096
    mio = dict(
        y_in=cio["y3"],
        wg=nc.dram_tensor("wg", [C, E], F32, kind="ExternalInput").ap(),
        w1=nc.dram_tensor("w1", [E, C, H], F32, kind="ExternalInput").ap(),
        w2m=nc.dram_tensor("w2m", [E, C, H], F32, kind="ExternalInput").ap(),
        w3=nc.dram_tensor("w3", [E, H, C], F32, kind="ExternalInput").ap(),
        ws1=nc.dram_tensor("ws1", [C, HS], F32, kind="ExternalInput").ap(),
        ws2=nc.dram_tensor("ws2", [C, HS], F32, kind="ExternalInput").ap(),
        ws3=nc.dram_tensor("ws3", [HS, C], F32, kind="ExternalInput").ap(),
        y_out=nc.dram_tensor("y_out", [T, C], F32, kind="ExternalOutput").ap(),
    )
    with ExitStack() as sT:
        P = Prog(nc, sT)
        cst = alloc_consts(nc, P, sT)
        pb = [P.ps(f"pb{i}", [128, 512], F32, sT) for i in range(7)]
        ptb = P.ps("ptb", [128, 1024], BF16, sT)
        emit_l1_c3(nc, P, cst, pb, ptb, cio)
        P.barrier()
        emit_moe_ffn(nc, P, cst, pb, ptb, 1, mio)
        P.drain()
        P.finalize()
    return nc


def kernel(**inputs):
    inp = inputs
    n = NCORES
    nc1 = build_stage1()
    in1 = []
    for b in range(n):
        m = prep_l0_attn_inputs(inp, b)
        moe = prep_moe_inputs(0, inp, None)
        del moe["y_in"]
        c1 = prep_c1_inputs(inp, b)
        del c1["y2"]
        m.update(moe)
        m.update(c1)
        in1.append(m)
    res1 = run_bass_kernel_spmd(nc1, in1, core_ids=list(range(n)))
    pg_all = np.ascontiguousarray(np.stack([res1.results[b]["pg_o"] for b in range(n)]))
    pv_all = np.ascontiguousarray(np.stack([res1.results[b]["pv_o"] for b in range(n)]))
    pp_all = np.ascontiguousarray(np.stack([res1.results[b]["pp_o"] for b in range(n)]))
    ncC2 = build_l1_attn_c2()
    projT0 = np.ascontiguousarray(inp["L1_attn_ttt_proj"].T)
    in2 = [dict(pg_all=pg_all, pv_all=pv_all, pp_all=pp_all,
                gate0=np.ascontiguousarray(inp["L1_attn_ttt_gate"]),
                val0=np.ascontiguousarray(inp["L1_attn_ttt_val"]), projT0=projT0,
                kn_i=res1.results[b]["kn_o"], vn_i=res1.results[b]["vn_o"])
           for b in range(n)]
    res2 = run_bass_kernel_spmd(ncC2, in2, core_ids=list(range(n)))
    pg2 = np.ascontiguousarray(np.stack([res2.results[b]["pg_o"] for b in range(n)]))
    pv2 = np.ascontiguousarray(np.stack([res2.results[b]["pv_o"] for b in range(n)]))
    pp2 = np.ascontiguousarray(np.stack([res2.results[b]["pp_o"] for b in range(n)]))
    nc3 = build_stage3()
    pjb = np.ascontiguousarray(inp["L1_attn_proj_b"].reshape(8, 128).T)
    in3 = []
    for b in range(n):
        m = dict(pg_all=pg2, pv_all=pv2, pp_all=pp2,
                 gate1=res2.results[0]["gate1"], val1=res2.results[0]["val1"],
                 projT1=res2.results[0]["projT1"], qn_i=res1.results[b]["qn_o"],
                 y2=np.ascontiguousarray(res1.results[b]["y2_out"]),
                 pw=np.ascontiguousarray(inp["L1_attn_proj_w"]), pjb=pjb)
        moe = prep_moe_inputs(1, inp, None)
        del moe["y_in"]
        m.update(moe)
        in3.append(m)
    res3 = run_bass_kernel_spmd(nc3, in3, core_ids=list(range(n)))
    out = np.stack([res3.results[b]["y_out"] for b in range(n)]).astype(np.float32)
    return out
